# version 01
# baseline (speedup 1.0000x reference)
import contextlib
import math
import numpy as np
import ml_dtypes
import concourse.bass as bass
import concourse.mybir as mybir
from concourse.bass_utils import run_bass_kernel_spmd
from concourse.alu_op_type import AluOpType as OP

F32 = mybir.dt.float32
BF16 = mybir.dt.bfloat16
U8 = mybir.dt.uint8
AF = mybir.ActivationFunctionType
bfdt = ml_dtypes.bfloat16

DM = 2048
INTER = 4096
HEADS = 64
PD = 64
CONV = 4352
KCONV = 4
LSLAB = 1024
HALO = 3
LT = LSLAB + HALO
NT = 8
NK = DM // 128
NKI = INTER // 128
Q = 128
NCH = 8
NWT = CONV // 128
EPS = 1e-5
RSQ8 = 1.0 / math.sqrt(8.0)

DEBUG = False


class Sem:
    def __init__(self, ctx, nc, name):
        self.h = ctx.enter_context(nc.semaphore(name))
        self.n = 0

    def inc(self, instr, k=1):
        instr.then_inc(self.h, k)
        self.n += k
        return self.n

    def dma(self, instr):
        return self.inc(instr, 16)

    def wait(self, eng, n=None):
        eng.wait_ge(self.h, self.n if n is None else n)


def _w(eng, sem):
    eng.wait_ge(sem.h, sem.n)
    return False


class Arena:
    def __init__(self, handle):
        self.h = handle

    def v(self, off_bytes, shape, dtype):
        es = mybir.dt.size(dtype)
        assert off_bytes % es == 0, (off_bytes, dtype)
        hv = self.h.bitcast(dtype)
        flat = int(np.prod(shape[1:]))
        p0, p1 = 0, shape[0]
        ap = hv.ap()[p0:p1, off_bytes // es: off_bytes // es + flat]
        if len(shape) == 2:
            return ap
        if len(shape) == 3:
            return ap.rearrange("p (a b) -> p a b", a=shape[1])
        if len(shape) == 4:
            return ap.rearrange("p (a b c) -> p a b c", a=shape[1], b=shape[2])
        raise ValueError(shape)


def host_consts():
    e = np.eye(128)
    t1 = np.tril(np.ones((128, 128)), -1)
    t1l = np.triu(np.ones((128, 128)), 0)
    return {
        "c_ident": e.astype(bfdt),
        "c_ident_f": e.astype(np.float32),
        "c_t1": t1.astype(bfdt),
        "c_t1l_f": t1l.astype(np.float32),
        "c_t1l_b": t1l.astype(bfdt),
        "c_ones_f": np.ones((128, 128), np.float32),
        "c_ones_b": np.ones((128, 128), bfdt),
    }



def build_l1(debug=False):
    ctx = contextlib.ExitStack()
    nc = bass.Bass("TRN2", target_bir_lowering=False, debug=False)

    dr = {}
    dr["x"] = nc.dram_tensor("x", [LT, DM], F32, kind="ExternalInput").ap()
    dr["wz"] = nc.dram_tensor("wz", [DM, INTER], F32, kind="ExternalInput").ap()
    dr["wx"] = nc.dram_tensor("wx", [DM, CONV], F32, kind="ExternalInput").ap()
    dr["wdt"] = nc.dram_tensor("wdt", [DM, HEADS], F32, kind="ExternalInput").ap()
    dr["convw"] = nc.dram_tensor("convw", [CONV, KCONV], F32, kind="ExternalInput").ap()
    dr["convb"] = nc.dram_tensor("convb", [CONV], F32, kind="ExternalInput").ap()
    dr["dtbias"] = nc.dram_tensor("dtbias", [HEADS], F32, kind="ExternalInput").ap()
    dr["alog"] = nc.dram_tensor("alog", [HEADS], F32, kind="ExternalInput").ap()
    dr["dvec"] = nc.dram_tensor("dvec", [HEADS], F32, kind="ExternalInput").ap()
    dr["fac"] = nc.dram_tensor("fac", [8, DM, DM], F32, kind="ExternalInput").ap()
    dr["fbias"] = nc.dram_tensor("fbias", [DM], F32, kind="ExternalInput").ap()
    for k, v in host_consts().items():
        dt_ = BF16 if v.dtype == bfdt else F32
        dr[k] = nc.dram_tensor(k, list(v.shape), dt_, kind="ExternalInput").ap()

    dr["e_s"] = nc.dram_tensor("e_s", [128, INTER], F32, kind="ExternalOutput").ap()
    dr["e_gam"] = nc.dram_tensor("e_gam", [128, HEADS], F32, kind="ExternalOutput").ap()
    dr["e_gbc"] = nc.dram_tensor("e_gbc", [128, NCH, HEADS], F32, kind="ExternalOutput").ap()
    dr["e_beta"] = nc.dram_tensor("e_beta", [128, NCH, HEADS], F32, kind="ExternalOutput").ap()
    dr["e_cc"] = nc.dram_tensor("e_cc", [128, LSLAB], BF16, kind="ExternalOutput").ap()
    dr["e_y"] = nc.dram_tensor("e_y", [NCH, 128, INTER], BF16, kind="ExternalOutput").ap()
    dr["e_sg"] = nc.dram_tensor("e_sg", [NT, 128, INTER], BF16, kind="ExternalOutput").ap()
    dr["e_g"] = nc.dram_tensor("e_g", [NT, 128, DM], BF16, kind="ExternalOutput").ap()
    dr["fsum"] = nc.dram_tensor("fsum", [NK, 128, DM], BF16, kind="Internal").ap()
    if debug:
        dr["d_xbc"] = nc.dram_tensor("d_xbc", [128, NWT, LT], BF16, kind="ExternalOutput").ap()
        dr["d_dt"] = nc.dram_tensor("d_dt", [128, NCH, HEADS], F32, kind="ExternalOutput").ap()
        dr["d_xnt"] = nc.dram_tensor("d_xnt", [128, NK, LT], BF16, kind="ExternalOutput").ap()
        dr["d_xs"] = nc.dram_tensor("d_xs", [NCH, 128, INTER], BF16, kind="ExternalOutput").ap()

    AR = 212480
    ar_h = ctx.enter_context(nc.sbuf_tensor("arena", [128, AR], U8))
    A = Arena(ar_h)
    o = 0

    def al(shape, dtype, align=64):
        nonlocal o
        es = mybir.dt.size(dtype)
        o = (o + align - 1) & ~(align - 1)
        t = A.v(o, shape, dtype)
        o += int(np.prod(shape[1:])) * es
        return t

    ident_b = al([128, 128], BF16)
    ident_f = al([128, 128], F32)
    t1_b = al([128, 128], BF16)
    t1l_f = al([128, 128], F32)
    t1l_b = al([128, 128], BF16)
    ones_f = al([128, 128], F32)
    ones_b = al([128, 128], BF16)
    negA = al([128, HEADS], F32)
    dbias_bc = al([128, HEADS], F32)
    d_bc = al([128, HEADS], F32)
    convw_s = al([128, NWT, KCONV], F32)
    cbc = al([128, 2], F32)
    mv = al([128, 8], F32)
    epsc = al([128, 1], F32)
    epsc = al([128, 1], F32)
    onec = al([128, 1], F32)
    bnbuf = al([128, 24], F32)
    sscr = al([128, HEADS], F32)
    S = al([128, INTER], F32)
    dt_f = al([128, NCH, HEADS], F32)
    dt_b = al([128, NCH, HEADS], BF16)
    dtA_f = al([128, NCH, HEADS], F32)
    dtA_b = al([128, NCH, HEADS], BF16)
    beta = al([128, NCH, HEADS], F32)
    gam = al([128, HEADS], F32)
    Bc = al([128, LSLAB], BF16)
    Cc = al([128, LSLAB], BF16)
    btw = al([128, 128], BF16)
    delta = al([128, HEADS], BF16)
    cb_hi = al([1, CONV], BF16)
    cb_lo = al([1, CONV], BF16)
    fb_hi = al([1, DM], BF16)
    fb_lo = al([1, DM], BF16)
    glcol = al([64, 1], F32)
    glrow = al([1, HEADS], F32)
    gbc = al([128, HEADS], F32)

    o = (o + 63) & ~63
    SEG_BIG = o; o += 69888
    SEG_MID = o; o += 34816
    SEG_F = o; o += 32768
    SEG_W1 = o; o += 2048
    SEG_W2 = o; o += 8448
    SEG_W3 = o; o += 2048
    assert o <= AR, o

    xbc = A.v(SEG_BIG, [128, NWT, LT], BF16)
    xnt = A.v(SEG_MID, [128, NK, LT], BF16)
    xs = A.v(SEG_MID, [128, 8, 512], BF16)
    Uu = A.v(SEG_MID + 8192, [128, HEADS, PD], BF16)
    ych = A.v(SEG_MID + 16384, [128, 8, 512], BF16)
    rtri = [A.v(SEG_MID + 24576 + i * 1024, [128, 4, 128], BF16) for i in range(2)]
    expg = [A.v(SEG_MID + 26624 + i * 1024, [128, 4, 128], BF16) for i in range(2)]
    gtm = A.v(SEG_MID + 28672, [128, 128], BF16)
    sbf = [A.v(SEG_MID + 28928 + i * 1024, [128, 512], BF16) for i in range(2)]
    stmp = A.v(SEG_MID + 30976, [128, 512], F32)
    ytmp = A.v(SEG_MID + 33024, [128, 512], BF16)

    xtile = [A.v(SEG_F + i * 8192, [128, DM], F32) for i in range(2)]
    xntile = [A.v(SEG_F + 16384 + i * 4096, [128, DM], BF16) for i in range(2)]
    srow = A.v(SEG_F, [1, CONV], F32)
    f_acc = [A.v(SEG_F + i * 8192, [128, DM], F32) for i in range(2)]
    f_bf = [A.v(SEG_F + 16384 + i * 4096, [128, DM], BF16) for i in range(2)]
    diags = A.v(SEG_F, [128, NKI, KCONV, 128], BF16)

    wzb = [A.v(SEG_W1 + i * 1024, [128, 512], BF16) for i in range(2)]
    fks = [A.v(SEG_W1 + i * 1024, [128, 512], BF16) for i in range(2)]
    wxb = [A.v(SEG_W2 + i * 4096, [128, NK, 128], BF16) for i in range(2)]
    sgsl = A.v(SEG_W2, [128, 8, 512], BF16)
    gtile = [A.v(SEG_W2 + i * 1024, [128, 512], BF16) for i in range(2)]
    convBCa = A.v(SEG_W2, [128, 1030], F32)
    wdtb = A.v(SEG_W3, [128, NK, HEADS], BF16)

    psh = [ctx.enter_context(nc.psum_tensor(f"ps{i}", [128, 512], F32)) for i in range(8)]
    ps = [h.ap() for h in psh]
    psB = [h.bitcast(BF16).ap() for h in psh]

    qG = Sem(ctx, nc, "qG")
    qF = [Sem(ctx, nc, "qF0"), Sem(ctx, nc, "qF1")]
    qI = Sem(ctx, nc, "qI")
    qC = Sem(ctx, nc, "qC")
    qX = [Sem(ctx, nc, "qX0"), Sem(ctx, nc, "qX1")]
    qZ = [Sem(ctx, nc, "qZ0"), Sem(ctx, nc, "qZ1")]
    qW = [Sem(ctx, nc, "qW0"), Sem(ctx, nc, "qW1")]
    qD = Sem(ctx, nc, "qD")
    qK = [Sem(ctx, nc, "qK0"), Sem(ctx, nc, "qK1")]
    qDB = Sem(ctx, nc, "qDB")
    qS = Sem(ctx, nc, "qS")
    qSG = Sem(ctx, nc, "qSG")
    qFS = Sem(ctx, nc, "qFS")
    qEG = Sem(ctx, nc, "qEG")
    qEY = Sem(ctx, nc, "qEY")
    qGB = Sem(ctx, nc, "qGB")
    qEX = Sem(ctx, nc, "qEX")
    tp = Sem(ctx, nc, "tp")
    vp = Sem(ctx, nc, "vp")
    sp = Sem(ctx, nc, "sp")
    bar = Sem(ctx, nc, "bar")

    blk = ctx.enter_context(nc.Block())

    def barrier():
        bar.n += 5
        tgt = bar.n
        own = {blk.tensor: tp, blk.vector: vp, blk.scalar: sp}
        for d in (blk.sync, blk.tensor, blk.vector, blk.scalar, blk.gpsimd):
            @d
            def _(e, tgt=tgt, d=d):
                s_ = own.get(d)
                if s_ is not None:
                    e.wait_ge(s_.h, s_.n)
                e.nop().then_inc(bar.h, 1)
                e.wait_ge(bar.h, tgt)

    tok = lambda t: slice(3 + 128 * t, 3 + 128 * t + 128)

    @blk.sync
    def _(sy):
        for name, dst in (("c_ident", ident_b), ("c_ident_f", ident_f), ("c_t1", t1_b),
                          ("c_t1l_f", t1l_f), ("c_t1l_b", t1l_b), ("c_ones_f", ones_f),
                          ("c_ones_b", ones_b)):
            qI.dma(sy.dma_start(out=dst, in_=dr[name]))
        qI.dma(sy.dma_start(out=convw_s, in_=dr["convw"].rearrange("(wt p) k -> p wt k", p=128)))
        qI.dma(sy.dma_start(out=srow[0:1, 0:HEADS], in_=dr["alog"][None, :]))
        qI.dma(sy.dma_start(out=srow[0:1, 128:128 + HEADS], in_=dr["dtbias"][None, :]))
        qI.dma(sy.dma_start(out=srow[0:1, 256:256 + HEADS], in_=dr["dvec"][None, :]))
        qI.dma(sy.dma_start(out=srow[0:1, 512:512 + DM], in_=dr["fbias"][None, :]))

    @blk.scalar
    def _(s):
        qI.wait(s)
        _w(s, sp) or sp.inc(s.activation(out=srow[0:1, 448:448 + HEADS], in_=srow[0:1, 0:HEADS], func=AF.Exp))

    @blk.tensor
    def _(t):
        sp.wait(t)
        tp.inc(t.matmul(ps[0][:, 0:64], ones_f[0:1, 0:128], srow[0:1, 448:448 + HEADS], start=True, stop=True))
        tp.inc(t.matmul(ps[0][:, 64:128], ones_f[0:1, 0:128], srow[0:1, 128:128 + HEADS], start=True, stop=True))
        tp.inc(t.matmul(ps[0][:, 128:192], ones_f[0:1, 0:128], srow[0:1, 256:256 + HEADS], start=True, stop=True))

    @blk.vector
    def _(v):
        tp.wait(v)
        _w(v, vp) or vp.inc(v.tensor_scalar(out=negA, in0=ps[0][:, 0:64], scalar1=-1.0, scalar2=None, op0=OP.mult))
        _w(v, vp) or vp.inc(v.tensor_copy(dbias_bc, ps[0][:, 64:128]))
        _w(v, vp) or vp.inc(v.tensor_copy(d_bc, ps[0][:, 128:192]))
        _w(v, vp) or vp.inc(v.tensor_scalar(out=srow[0:1, 512:512 + DM], in0=srow[0:1, 512:512 + DM],
                               scalar1=float(math.sqrt(8.0)), scalar2=None, op0=OP.mult))
        _w(v, vp) or vp.inc(v.tensor_copy(fb_hi, srow[0:1, 512:512 + DM]))
        _w(v, vp) or vp.inc(v.tensor_tensor(out=fb_lo, in0=srow[0:1, 512:512 + DM], in1=fb_hi, op=OP.subtract))
        _w(v, vp) or vp.inc(v.memset(S, 0.0))
        _w(v, vp) or vp.inc(v.memset(gam, 1.0))
        _w(v, vp) or vp.inc(v.memset(epsc, EPS))
        _w(v, vp) or vp.inc(v.memset(onec, 1.0))

    @blk.sync
    def _(sy):
        vp.wait(sy)
        qC.dma(sy.dma_start(out=srow[0:1, :], in_=dr["convb"][None, :]))

    @blk.vector
    def _(v):
        qC.wait(v)
        _w(v, vp) or vp.inc(v.tensor_copy(cb_hi, srow[0:1, :]))
        _w(v, vp) or vp.inc(v.tensor_tensor(out=cb_lo, in0=srow[0:1, :], in1=cb_hi, op=OP.subtract))

    @blk.tensor
    def _(t):
        vp.wait(t)
        tp.inc(t.transpose(psB[1][:, 0:1], cb_hi[0:1, 4096:4224], ident_b[0:1, 0:1]))
        tp.inc(t.transpose(psB[1][:, 2:3], cb_hi[0:1, 4224:4352], ident_b[0:1, 0:1]))
        tp.inc(t.transpose(psB[1][:, 4:5], cb_lo[0:1, 4096:4224], ident_b[0:1, 0:1]))
        tp.inc(t.transpose(psB[1][:, 6:7], cb_lo[0:1, 4224:4352], ident_b[0:1, 0:1]))

    @blk.scalar
    def _(s):
        tp.wait(s)
        _w(s, sp) or sp.inc(s.copy(out=cbc[:, 0:1], in_=psB[1][:, 0:1]))
        _w(s, sp) or sp.inc(s.copy(out=cbc[:, 1:2], in_=psB[1][:, 2:3]))

    @blk.vector
    def _(v):
        sp.wait(v)
        _w(v, vp) or vp.inc(v.tensor_tensor(out=cbc[:, 0:1], in0=cbc[:, 0:1], in1=psB[1][:, 4:5], op=OP.add))
        _w(v, vp) or vp.inc(v.tensor_tensor(out=cbc[:, 1:2], in0=cbc[:, 1:2], in1=psB[1][:, 6:7], op=OP.add))

    barrier()

    tiles = [(3, 0)] + [(128, 3 + 128 * i) for i in range(NT)]
    vn_xn = {}
    trans_done = {}
    ev_done = {}
    for i, (rows, coff) in enumerate(tiles):
        pp = i % 2
        bk = 2 * (i % 2)

        @blk.sync
        def _(sy, i=i, pp=pp, rows=rows, coff=coff):
            if i >= 2:
                sy.wait_ge(vp.h, vn_xn[i - 2])
            qX[pp].dma(sy.dma_start(out=xtile[pp][0:rows, :], in_=dr["x"][coff:coff + rows, :]))

        @blk.vector
        def _(v, i=i, pp=pp, rows=rows):
            qX[pp].wait(v)
            for j in range(4):
                _w(v, vp) or vp.inc(v.bn_stats(out=bnbuf[0:rows, 6 * j:6 * j + 6], in_=xtile[pp][0:rows, 512 * j:512 * j + 512]))
            _w(v, vp) or vp.inc(v.bn_aggr(out=mv[0:rows, 0:2], in_=bnbuf[0:rows, 0:24]))

        @blk.scalar
        def _(s, rows=rows):
            vp.wait(s)
            _w(s, sp) or sp.inc(s.activation(out=mv[0:rows, 3:4], in_=mv[0:rows, 1:2], func=AF.Ln, bias=epsc[0:rows, :]))
            _w(s, sp) or sp.inc(s.activation(out=mv[0:rows, 2:3], in_=mv[0:rows, 3:4], func=AF.Exp, scale=-0.5))

        @blk.vector
        def _(v, i=i, pp=pp, rows=rows):
            sp.wait(v)
            if i >= 2:
                v.wait_ge(tp.h, trans_done[i - 2])
            vn_xn[i] = _w(v, vp) or vp.inc(v.tensor_scalar(out=xntile[pp][0:rows, :], in0=xtile[pp][0:rows, :],
                                              scalar1=mv[0:rows, 0:1], scalar2=mv[0:rows, 2:3],
                                              op0=OP.subtract, op1=OP.mult))

        @blk.tensor
        def _(t, i=i, pp=pp, rows=rows, bk=bk):
            t.wait_ge(vp.h, vn_xn[i])
            if i >= 2:
                t.wait_ge(sp.h, ev_done[i - 2])
            for k in range(NK):
                b = bk + k // 8
                tp.inc(t.transpose(psB[b][:, (k % 8) * 128:(k % 8) * 128 + rows],
                                   xntile[pp][0:rows, 128 * k:128 * k + 128], ident_b[0:rows, 0:rows]))
            trans_done[i] = tp.n

        @blk.scalar
        def _(s, i=i, rows=rows, coff=coff, bk=bk):
            tp.wait(s)
            for half in range(2):
                src = psB[bk + half].rearrange("p (a c) -> p a c", a=8)[:, :, 0:rows]
                _w(s, sp) or sp.inc(s.copy(out=xnt[:, 8 * half:8 * half + 8, coff:coff + rows], in_=src))
            ev_done[i] = sp.n

    barrier()

    if debug:
        @blk.sync
        def _(sy):
            qDB.dma(sy.dma_start(out=dr["d_xnt"], in_=xnt))

    f_steps = []
    f_conv_done = {}
    f_chain = {r: 0 for r in range(2)}

    def make_f_step(kk, r):
        def step(g):
            reg = kk % 2
            if r == 0:
                if kk >= 2:
                    g.wait_ge(vp.h, f_conv_done[kk - 2])
                qF[reg].dma(g.dma_start(out=f_acc[reg], in_=dr["fac"][0, 128 * kk:128 * kk + 128, :]))
            else:
                qF[reg].wait(g)
                qF[reg].dma(g.dma_start(out=f_acc[reg], in_=dr["fac"][r, 128 * kk:128 * kk + 128, :],
                                        accum_op=OP.add))
        return step

    for kk in range(NK):
        for r in range(8):
            f_steps.append((kk, r, make_f_step(kk, r)))

    f_emit_idx = [0]
    f_conv_emitted = [0]
    f_spill = {}

    def emit_f_steps(g, n):
        while n > 0 and f_emit_idx[0] < len(f_steps):
            kk, r, step = f_steps[f_emit_idx[0]]
            step(g)
            f_emit_idx[0] += 1
            n -= 1

    def flush_f_convs():
        done_chains = f_emit_idx[0] // 8
        while f_conv_emitted[0] < done_chains:
            kk = f_conv_emitted[0]
            reg = kk % 2

            @blk.vector
            def _(v, kk=kk, reg=reg):
                v.wait_ge(qF[reg].h, 16 * 8 * (kk // 2 + 1))
                v.wait_ge(qFS.h, qFS.n)
                f_conv_done[kk] = _w(v, vp) or vp.inc(v.tensor_copy(f_bf[reg], f_acc[reg]))

            @blk.scalar
            def _(s, kk=kk, reg=reg):
                vp.wait(s)
                _w(s, sp) or qFS.dma(s.dma_start(out=dr["fsum"][kk], in_=f_bf[reg]))
                f_spill[kk] = qFS.n

            f_conv_emitted[0] += 1

    wz_used = {}
    for oo in range(8):
        for k in range(NK):
            idx = oo * NK + k
            pp = idx % 2

            @blk.gpsimd
            def _(g, oo=oo, k=k, pp=pp, idx=idx):
                emit_f_steps(g, 1)
                if idx >= 2:
                    g.wait_ge(tp.h, wz_used[idx - 2])
                qZ[pp].dma(g.dma_start(out=wzb[pp], in_=dr["wz"][128 * k:128 * k + 128, 512 * oo:512 * oo + 512]))

            @blk.tensor
            def _(t, oo=oo, k=k, pp=pp, idx=idx):
                qZ[pp].wait(t)
                if k == 0 and oo > 0:
                    t.wait_ge(sp.h, ev_done[("g", oo - 1)])
                mm = None
                for tt in range(NT):
                    mm = t.matmul(ps[tt], xnt[:, k, tok(tt)], wzb[pp], start=(k == 0), stop=(k == NK - 1))
                wz_used[idx] = tp.inc(mm)

        @blk.scalar
        def _(s, oo=oo):
            tp.wait(s)
            if oo >= 1:
                qSG.wait(s)
            for tt in range(NT):
                _w(s, sp) or sp.inc(s.activation(out=sgsl[:, tt, :], in_=ps[tt], func=AF.Silu))
            ev_done[("g", oo)] = sp.n
            _w(s, sp) or qSG.dma(s.dma_start(out=dr["e_sg"].rearrange("t p c -> p t c")[:, :, 512 * oo:512 * oo + 512],
                               in_=sgsl))

        flush_f_convs()

    barrier()

    wx_used = {}
    groups = [(0, 512), (512, 512), (1024, 515 - 512)]
    for wt in range(NWT):
        pp = wt % 2
        bk0 = 3 * (wt % 2)

        @blk.gpsimd
        def _(g, wt=wt, pp=pp):
            emit_f_steps(g, 2)
            if wt < 2:
                g.wait_ge(qSG.h, qSG.n)
            if wt >= 2:
                g.wait_ge(tp.h, wx_used[wt - 2])
            srcw = dr["wx"].rearrange("(k p) c -> p k c", p=128)
            qW[pp].dma(g.dma_start(out=wxb[pp][:, 0:8, :], in_=srcw[:, 0:8, 128 * wt:128 * wt + 128]))
            qW[pp].dma(g.dma_start(out=wxb[pp][:, 8:16, :], in_=srcw[:, 8:16, 128 * wt:128 * wt + 128]))

        @blk.tensor
        def _(t, wt=wt, pp=pp, bk0=bk0):
            qW[pp].wait(t)
            if wt >= 2:
                t.wait_ge(sp.h, ev_done[("x", wt - 2)])
            mm = None
            for gi, (goff, glen) in enumerate(groups):
                for k in range(NK):
                    mm = t.matmul(ps[bk0 + gi][:, 0:glen], wxb[pp][:, k, :], xnt[:, k, goff:goff + glen],
                                  start=(k == 0), stop=(k == NK - 1))
            wx_used[wt] = tp.inc(mm)

        @blk.scalar
        def _(s, wt=wt, bk0=bk0):
            tp.wait(s)
            for gi, (goff, glen) in enumerate(groups):
                _w(s, sp) or sp.inc(s.copy(out=xbc[:, wt, goff:goff + glen], in_=ps[bk0 + gi][:, 0:glen]))
            ev_done[("x", wt)] = sp.n

        flush_f_convs()

    barrier()
    if debug:
        @blk.sync
        def _(sy):
            qDB.dma(sy.dma_start(out=dr["d_xbc"], in_=xbc))

    @blk.gpsimd
    def _(g):
        emit_f_steps(g, 1000)
        srcw = dr["wdt"].rearrange("(k p) h -> p k h", p=128)
        qD.dma(g.dma_start(out=wdtb[:, 0:8, :], in_=srcw[:, 0:8, :]))
        qD.dma(g.dma_start(out=wdtb[:, 8:16, :], in_=srcw[:, 8:16, :]))

    @blk.tensor
    def _(t):
        qD.wait(t)
        mm = None
        for tt in range(NT):
            for k in range(NK):
                mm = t.matmul(ps[tt][:, 0:HEADS], xnt[:, k, tok(tt)], wdtb[:, k, :],
                              start=(k == 0), stop=(k == NK - 1))
        tp.inc(mm)

    @blk.vector
    def _(v):
        tp.wait(v)
        for tt in range(NT):
            _w(v, vp) or vp.inc(v.scalar_tensor_tensor(out=dt_f[:, tt, :], in0=ps[tt][:, 0:HEADS], scalar=40.0,
                                          in1=dbias_bc, op0=OP.min, op1=OP.add))

    @blk.scalar
    def _(s):
        vp.wait(s)
        for tt in range(NT):
            _w(s, sp) or sp.inc(s.activation(out=sscr, in_=dt_f[:, tt, :], func=AF.Exp))
            _w(s, sp) or sp.inc(s.activation(out=dt_f[:, tt, :], in_=sscr, func=AF.Ln, bias=onec))

    flush_f_convs()

    @blk.vector
    def _(v):
        sp.wait(v)
        _w(v, vp) or vp.inc(v.tensor_copy(dt_b, dt_f))
        _w(v, vp) or vp.inc(v.tensor_tensor(out=dtA_f, in0=dt_f,
                               in1=negA[:, None, :].broadcast_to([128, NCH, HEADS]), op=OP.mult))
        _w(v, vp) or vp.inc(v.tensor_copy(dtA_b, dtA_f))

    barrier()
    if debug:
        @blk.sync
        def _(sy):
            qDB.dma(sy.dma_start(out=dr["d_dt"], in_=dt_f))

    barrier()

    @blk.sync
    def _(sy):
        qFS.wait(sy)

    fk_used = {}
    for oo in range(4):
        for k in range(NK):
            idx = oo * NK + k
            pp = idx % 2

            @blk.sync
            def _(sy, oo=oo, k=k, pp=pp, idx=idx):
                if idx >= 2:
                    sy.wait_ge(tp.h, fk_used[idx - 2])
                qK[pp].dma(sy.dma_start(out=fks[pp], in_=dr["fsum"][k][:, 512 * oo:512 * oo + 512]))

            @blk.tensor
            def _(t, oo=oo, k=k, pp=pp, idx=idx):
                qK[pp].wait(t)
                if k == 0 and oo > 0:
                    t.wait_ge(sp.h, ev_done[("f", oo - 1)])
                mm = None
                for tt in range(NT):
                    mm = t.matmul(ps[tt], xnt[:, k, tok(tt)], fks[pp], start=(k == 0), stop=False)
                fk_used[idx] = tp.inc(mm)

        @blk.tensor
        def _(t, oo=oo):
            mm = None
            for tt in range(NT):
                t.matmul(ps[tt], ones_b[0:1, 0:128], fb_hi[0:1, 512 * oo:512 * oo + 512], start=False, stop=False)
                mm = t.matmul(ps[tt], ones_b[0:1, 0:128], fb_lo[0:1, 512 * oo:512 * oo + 512], start=False, stop=True)
            tp.inc(mm)

        @blk.scalar
        def _(s, oo=oo):
            tp.wait(s)
            for tt in range(NT):
                pp = tt % 2
                if tt >= 2 or oo > 0:
                    qEG.wait(s)
                _w(s, sp) or sp.inc(s.activation(out=gtile[pp], in_=ps[tt], func=AF.Silu, scale=RSQ8))
                _w(s, sp) or qEG.dma(s.dma_start(out=dr["e_g"][tt][:, 512 * oo:512 * oo + 512], in_=gtile[pp]))
            ev_done[("f", oo)] = sp.n

    barrier()

    @blk.vector
    def _(v):
        v.wait_ge(qEG.h, qEG.n)
        v.wait_ge(qDB.h, qDB.n)
        for cs in range(NKI):
            _w(v, vp) or vp.inc(v.tensor_tensor(
                out=diags[:, cs, :, :],
                in0=ident_b[:, None, :].broadcast_to([128, KCONV, 128]),
                in1=convw_s[:, cs, :, None].broadcast_to([128, KCONV, 128]),
                op=OP.mult))
        for ci, wt in enumerate((32, 33)):
            dst = Bc if ci == 0 else Cc
            _w(v, vp) or vp.inc(v.tensor_scalar(out=convBCa[:, 0:LSLAB], in0=xbc[:, wt, 0:LSLAB],
                                   scalar1=convw_s[:, wt, 0:1], scalar2=None, op0=OP.mult))
            for k in range(1, KCONV):
                _w(v, vp) or vp.inc(v.scalar_tensor_tensor(out=convBCa[:, 0:LSLAB], in0=xbc[:, wt, k:k + LSLAB],
                                              scalar=convw_s[:, wt, k:k + 1], in1=convBCa[:, 0:LSLAB],
                                              op0=OP.mult, op1=OP.add))
            _w(v, vp) or vp.inc(v.nop())
            ev_done[("bc", ci)] = vp.n

    @blk.scalar
    def _(s):
        vp.wait(s)
        pass

    for ci, wt in enumerate((32, 33)):
        dst = Bc if ci == 0 else Cc

        @blk.vector
        def _(v, ci=ci, wt=wt):
            if ci == 1:
                sp.wait(v)
            _w(v, vp) or vp.inc(v.tensor_scalar(out=convBCa[:, 0:LSLAB], in0=xbc[:, wt, 0:LSLAB],
                                   scalar1=convw_s[:, wt, 0:1], scalar2=None, op0=OP.mult))
            for k in range(1, KCONV):
                _w(v, vp) or vp.inc(v.scalar_tensor_tensor(out=convBCa[:, 0:LSLAB], in0=xbc[:, wt, k:k + LSLAB],
                                              scalar=convw_s[:, wt, k:k + 1], in1=convBCa[:, 0:LSLAB],
                                              op0=OP.mult, op1=OP.add))

        @blk.scalar
        def _(s, ci=ci, dst=dst):
            vp.wait(s)
            _w(s, sp) or sp.inc(s.activation(out=dst, in_=convBCa[:, 0:LSLAB], func=AF.Silu, bias=cbc[:, ci:ci + 1]))

    @blk.scalar
    def _(s):
        _w(s, sp) or qEX.dma(s.dma_start(out=dr["e_cc"], in_=Cc))

    barrier()

    for q in range(NCH):
        qq = 128 * q

        @blk.tensor
        def _(t, q=q):
            vp.wait(t)
            tp.inc(t.matmul(ps[7][:, 0:HEADS], t1l_f, dtA_f[:, q, :], start=True, stop=True))

        @blk.scalar
        def _(s, q=q):
            tp.wait(s)
            _w(s, sp) or sp.inc(s.activation(out=beta[:, q, :], in_=ps[7][:, 0:HEADS], func=AF.Exp))

        @blk.tensor
        def _(t, q=q):
            sp.wait(t)
            tp.inc(t.matmul(ps[7][0:64, 64:65], dtA_f[:, q, :], ones_f[:, 0:1], start=True, stop=True))
            tp.inc(t.matmul(ps[2][:, 0:128], Bc[:, qq:qq + 128], Cc[:, qq:qq + 128], start=True, stop=True))
            tp.inc(t.transpose(psB[2][:, 256:384], Bc[:, qq:qq + 128], ident_b))

        @blk.scalar
        def _(s, q=q):
            tp.wait(s)
            _w(s, sp) or sp.inc(s.copy(out=glcol, in_=ps[7][0:64, 64:65]))

        @blk.tensor
        def _(t, q=q):
            sp.wait(t)
            tp.inc(t.transpose(ps[7][0:1, 128:192], glcol, ident_f[0:64, 0:64]))

        @blk.scalar
        def _(s, q=q):
            tp.wait(s)
            _w(s, sp) or sp.inc(s.copy(out=glrow, in_=ps[7][0:1, 128:192]))

        @blk.tensor
        def _(t, q=q):
            sp.wait(t)
            tp.inc(t.matmul(ps[7][:, 192:256], ones_f[0:1, 0:128], glrow, start=True, stop=True))

        @blk.scalar
        def _(s, q=q):
            tp.wait(s)
            s.wait_ge(qGB.h, qGB.n)
            _w(s, sp) or sp.inc(s.activation(out=gbc, in_=ps[7][:, 192:256], func=AF.Exp))

        @blk.vector
        def _(v, q=q):
            sp.wait(v)
            _w(v, vp) or vp.inc(v.tensor_tensor(out=gam, in0=gam, in1=gbc, op=OP.mult))
            _w(v, vp) or vp.inc(v.tensor_tensor(out=gtm, in0=ps[2][:, 0:128], in1=t1l_b, op=OP.mult))

        @blk.scalar
        def _(s, q=q):
            vp.wait(s)
            _w(s, sp) or qGB.dma(s.dma_start(out=dr["e_gbc"][:, q, :], in_=gbc))

        @blk.scalar
        def _(s, q=q):
            vp.wait(s)
            _w(s, sp) or sp.inc(s.copy(out=btw, in_=psB[2][:, 256:384]))
            ev_done[("btw", q)] = sp.n

        for b8 in range(8):
            bk = 3 + (b8 % 2)

            @blk.tensor
            def _(t, q=q, b8=b8, bk=bk, qq=qq):
                if b8 >= 2:
                    t.wait_ge(vp.h, ev_done[("u", q, b8 - 2)])
                first = True
                for csl in range(4):
                    cs = 4 * b8 + csl
                    for k in range(KCONV):
                        t.matmul(ps[bk][:, 128 * csl:128 * csl + 128],
                                 xbc[:, cs, qq + k:qq + k + 128], diags[:, cs, k, :],
                                 start=first, stop=False)
                        first = False
                t.matmul(ps[bk], ones_b[0:1, 0:128], cb_hi[0:1, 512 * b8:512 * b8 + 512], start=False, stop=False)
                mm = t.matmul(ps[bk], ones_b[0:1, 0:128], cb_lo[0:1, 512 * b8:512 * b8 + 512], start=False, stop=True)
                tp.inc(mm)

            @blk.scalar
            def _(s, q=q, b8=b8, bk=bk):
                tp.wait(s)
                _w(s, sp) or sp.inc(s.activation(out=xs[:, b8, :], in_=ps[bk], func=AF.Silu))

            @blk.vector
            def _(v, q=q, b8=b8):
                sp.wait(v)
                uv = Uu.rearrange("p h d -> p (h d)").rearrange("p (a c) -> p a c", a=8)
                _w(v, vp) or vp.inc(v.tensor_tensor(
                    out=uv[:, b8, :].rearrange("p (h d) -> p h d", h=8),
                    in0=xs[:, b8, :].rearrange("p (h d) -> p h d", h=8),
                    in1=dt_b[:, q, 8 * b8:8 * b8 + 8, None].broadcast_to([128, 8, PD]), op=OP.mult))
                _w(v, vp) or vp.inc(v.tensor_tensor(
                    out=xs[:, b8, :].rearrange("p (h d) -> p h d", h=8),
                    in0=xs[:, b8, :].rearrange("p (h d) -> p h d", h=8),
                    in1=d_bc[:, 8 * b8:8 * b8 + 8, None].broadcast_to([128, 8, PD]), op=OP.mult))
                ev_done[("u", q, b8)] = vp.n

        if debug and q == 0:
            @blk.sync
            def _(sy):
                vp.wait(sy)
                qDB.dma(sy.dma_start(out=dr["d_xs"][0], in_=Uu.rearrange("p h d -> p (h d)")))

        for g8 in range(8):
            pa = 5 + (g8 % 2)
            pb = 3 + (g8 % 2)
            for gi in range(2):
                gg = 2 * g8 + gi
                pp = gg % 2

                @blk.vector
                def _(v, q=q, gg=gg, pp=pp, g8=g8):
                    if gg >= 2:
                        tp.wait(v, ev_done[("dt_mm", q, gg - 2)])
                    _w(v, vp) or vp.inc(v.tensor_tensor(
                        out=rtri[pp],
                        in0=t1l_b[:, None, :].broadcast_to([128, 4, 128]),
                        in1=dtA_b[:, q, 4 * gg:4 * gg + 4, None].broadcast_to([128, 4, 128]),
                        op=OP.mult))

                @blk.tensor
                def _(t, q=q, gg=gg, pp=pp):
                    vp.wait(t)
                    if gg >= 2:
                        t.wait_ge(sp.h, ev_done[("exp", q, gg - 2)])
                    ev_done[("dt_mm", q, gg)] = tp.inc(
                        t.matmul(ps[gg % 2], t1_b, rtri[pp].rearrange("p a c -> p (a c)"),
                                 start=True, stop=True))

                @blk.scalar
                def _(s, q=q, gg=gg, pp=pp):
                    tp.wait(s)
                    ev_done[("exp", q, gg)] = sp.inc(
                        s.activation(out=expg[pp].rearrange("p a c -> p (a c)"), in_=ps[gg % 2], func=AF.Exp))

                @blk.vector
                def _(v, q=q, gg=gg, pp=pp):
                    sp.wait(v)
                    _w(v, vp) or vp.inc(v.tensor_copy(delta[:, 4 * gg:4 * gg + 4], expg[pp][:, :, 127]))
                    _w(v, vp) or vp.inc(v.tensor_tensor(out=expg[pp], in0=expg[pp],
                                           in1=gtm[:, None, :].broadcast_to([128, 4, 128]), op=OP.mult))
                    ev_done[("mt", q, gg)] = vp.n

            @blk.vector
            def _(v, q=q, g8=g8):
                if q > 0:
                    _w(v, vp) or vp.inc(v.tensor_copy(sbf[g8 % 2], S[:, 512 * g8:512 * g8 + 512]))

            @blk.tensor
            def _(t, q=q, g8=g8, pa=pa, pb=pb):
                vp.wait(t)
                if g8 >= 2:
                    t.wait_ge(vp.h, ev_done[("yev", q, g8 - 2)])
                for j8 in range(8):
                    h = 8 * g8 + j8
                    t.matmul(ps[pa][:, PD * j8:PD * j8 + PD], expg[(2 * g8 + j8 // 4) % 2][:, h % 4, :],
                             Uu[:, h, :], start=(j8 == 0), stop=False)
                mm = t.matmul(ps[pa], ident_b, xs[:, g8, :], start=False, stop=True)
                if q > 0:
                    mm = t.matmul(ps[pb], Cc[:, qq:qq + 128], sbf[g8 % 2], start=True, stop=True)
                tp.inc(mm)

            @blk.vector
            def _(v, q=q, g8=g8, pa=pa, pb=pb):
                tp.wait(v)
                if q > 0:
                    _w(v, vp) or vp.inc(v.tensor_tensor(
                        out=ytmp.rearrange("p (a c) -> p a c", a=8),
                        in0=ps[pb].rearrange("p (a c) -> p a c", a=8),
                        in1=beta[:, q, 8 * g8:8 * g8 + 8, None].broadcast_to([128, 8, PD]), op=OP.mult))
                    _w(v, vp) or vp.inc(v.tensor_tensor(out=ych[:, g8, :], in0=ps[pa], in1=ytmp, op=OP.add))
                else:
                    _w(v, vp) or vp.inc(v.tensor_copy(ych[:, g8, :], ps[pa]))
                ev_done[("yev", q, g8)] = vp.n

        @blk.vector
        def _(v, q=q):
            v.wait_ge(qDB.h, qDB.n)
            _w(v, vp) or vp.inc(v.tensor_tensor(out=Uu, in0=Uu,
                                   in1=delta[:, :, None].broadcast_to([128, HEADS, PD]), op=OP.mult))

        for g8 in range(8):
            pS = g8 % 2

            @blk.tensor
            def _(t, q=q, g8=g8, pS=pS):
                vp.wait(t)
                t.wait_ge(sp.h, ev_done[("btw", q)])
                if g8 >= 2:
                    t.wait_ge(vp.h, ev_done[("sup", q, g8 - 2)])
                tp.inc(t.matmul(ps[pS], btw, Uu.rearrange("p h d -> p (h d)")[:, 512 * g8:512 * g8 + 512],
                                start=True, stop=True))

            @blk.vector
            def _(v, q=q, g8=g8, pS=pS):
                tp.wait(v)
                _w(v, vp) or vp.inc(v.tensor_tensor(
                    out=stmp.rearrange("p (a c) -> p a c", a=8),
                    in0=S[:, 512 * g8:512 * g8 + 512].rearrange("p (a c) -> p a c", a=8),
                    in1=gbc[:, 8 * g8:8 * g8 + 8, None].broadcast_to([128, 8, PD]), op=OP.mult))
                _w(v, vp) or vp.inc(v.tensor_tensor(out=S[:, 512 * g8:512 * g8 + 512], in0=stmp, in1=ps[pS], op=OP.add))
                ev_done[("sup", q, g8)] = vp.n

        @blk.scalar
        def _(s, q=q):
            s.wait_ge(vp.h, ev_done[("yev", q, 7)])
            _w(s, sp) or qEY.dma(s.dma_start(out=dr["e_y"][q], in_=ych.rearrange("p a c -> p (a c)")))

        @blk.vector
        def _(v, q=q):
            qEY.wait(v)

    @blk.scalar
    def _(s):
        vp.wait(s)
        _w(s, sp) or qEX.dma(s.dma_start(out=dr["e_s"], in_=S))
        _w(s, sp) or qEX.dma(s.dma_start(out=dr["e_gam"], in_=gam))
        _w(s, sp) or qEX.dma(s.dma_start(out=dr["e_beta"], in_=beta))

    @blk.sync
    def _(sy):
        qEX.wait(sy)
        qEY.wait(sy)
        qSG.wait(sy)
        qEG.wait(sy)
        qGB.wait(sy)
        qDB.wait(sy)

    barrier()
    ctx.close()
    return nc



def build_l2():
    ctx = contextlib.ExitStack()
    nc = bass.Bass("TRN2", target_bir_lowering=False, debug=False)

    dr = {}
    dr["x"] = nc.dram_tensor("x", [LSLAB, DM], F32, kind="ExternalInput").ap()
    dr["wout"] = nc.dram_tensor("wout", [INTER, DM], F32, kind="ExternalInput").ap()
    dr["nw"] = nc.dram_tensor("nw", [INTER], F32, kind="ExternalInput").ap()
    dr["s_all"] = nc.dram_tensor("s_all", [8, 128, INTER], F32, kind="ExternalInput").ap()
    dr["gam_all"] = nc.dram_tensor("gam_all", [8, 128, HEADS], F32, kind="ExternalInput").ap()
    dr["mask"] = nc.dram_tensor("mask", [128, 16], F32, kind="ExternalInput").ap()
    dr["gbc_all"] = nc.dram_tensor("gbc_all", [128, NCH, HEADS], F32, kind="ExternalInput").ap()
    dr["beta"] = nc.dram_tensor("beta", [128, NCH, HEADS], F32, kind="ExternalInput").ap()
    dr["cc"] = nc.dram_tensor("cc", [128, LSLAB], BF16, kind="ExternalInput").ap()
    dr["y_sp"] = nc.dram_tensor("y_sp", [NCH, 128, INTER], BF16, kind="ExternalInput").ap()
    dr["sg_sp"] = nc.dram_tensor("sg_sp", [NT, 128, INTER], BF16, kind="ExternalInput").ap()
    dr["g_sp"] = nc.dram_tensor("g_sp", [NT, 128, DM], BF16, kind="ExternalInput").ap()
    for k, v in host_consts().items():
        dt_ = BF16 if v.dtype == bfdt else F32
        dr[k] = nc.dram_tensor(k, list(v.shape), dt_, kind="ExternalInput").ap()
    dr["out"] = nc.dram_tensor("out", [LSLAB, DM], F32, kind="ExternalOutput").ap()

    AR = 212480
    ar_h = ctx.enter_context(nc.sbuf_tensor("arena", [128, AR], U8))
    A = Arena(ar_h)
    o = 0

    def al(shape, dtype, align=64):
        nonlocal o
        es = mybir.dt.size(dtype)
        o = (o + align - 1) & ~(align - 1)
        t = A.v(o, shape, dtype)
        o += int(np.prod(shape[1:])) * es
        return t

    ident_b = al([128, 128], BF16)
    ident_f = al([128, 128], F32)
    ones_f = al([128, 128], F32)
    beta_s = al([128, NCH, HEADS], F32)
    gbc_s = al([128, NCH, HEADS], F32)
    gamp = al([128, HEADS], F32)
    wq = al([128, HEADS], F32)
    gmj = al([128, HEADS], F32)
    maskc = al([128, 16], F32)
    mv = al([128, 8], F32)
    epsc = al([128, 1], F32)
    epsc = al([128, 1], F32)
    onec = al([128, 1], F32)
    cc_s = al([128, LSLAB], BF16)
    nw32 = al([32, 128], F32)
    nw_s = al([128, 32], F32)
    S_in = al([128, INTER], F32)
    S_bf = al([128, INTER], BF16)
    sj = [al([128, INTER], F32) for i in range(2)]
    gj = [al([128, HEADS], F32) for i in range(2)]
    normedT = al([128, NKI, LSLAB], BF16)
    y_t = [al([128, INTER], BF16) for i in range(2)]
    sg_t = [al([128, INTER], BF16) for i in range(2)]
    scr = al([128, INTER], BF16)
    wob = [al([128, 512], BF16) for i in range(2)]
    x_sl = [al([128, 512], F32) for i in range(2)]
    g_sl = [al([128, 512], BF16) for i in range(2)]
    r_sl = [al([128, 512], F32) for i in range(2)]
    o_sl = [al([128, 512], F32) for i in range(2)]
    assert o <= AR, o

    psh = [ctx.enter_context(nc.psum_tensor(f"ps{i}", [128, 512], F32)) for i in range(8)]
    ps = [h.ap() for h in psh]
    psB = [h.bitcast(BF16).ap() for h in psh]

    qG = Sem(ctx, nc, "qG")
    qI = Sem(ctx, nc, "qI")
    qA = [Sem(ctx, nc, "qA0"), Sem(ctx, nc, "qA1")]
    qY = [Sem(ctx, nc, "qY0"), Sem(ctx, nc, "qY1")]
    qV = [Sem(ctx, nc, "qV0"), Sem(ctx, nc, "qV1")]
    qR = [Sem(ctx, nc, "qR0"), Sem(ctx, nc, "qR1")]
    qO = [Sem(ctx, nc, "qO0"), Sem(ctx, nc, "qO1")]
    tp = Sem(ctx, nc, "tp")
    vp = Sem(ctx, nc, "vp")
    sp = Sem(ctx, nc, "sp")
    gpp = Sem(ctx, nc, "gpp")
    bar = Sem(ctx, nc, "bar")

    blk = ctx.enter_context(nc.Block())

    def barrier():
        bar.n += 5
        tgt = bar.n
        own = {blk.tensor: tp, blk.vector: vp, blk.scalar: sp, blk.gpsimd: gpp}
        for d in (blk.sync, blk.tensor, blk.vector, blk.scalar, blk.gpsimd):
            @d
            def _(e, tgt=tgt, d=d):
                s_ = own.get(d)
                if s_ is not None:
                    e.wait_ge(s_.h, s_.n)
                e.nop().then_inc(bar.h, 1)
                e.wait_ge(bar.h, tgt)

    ev = {}

    @blk.sync
    def _(sy):
        qI.dma(sy.dma_start(out=ident_b, in_=dr["c_ident"]))
        qI.dma(sy.dma_start(out=ident_f, in_=dr["c_ident_f"]))
        qI.dma(sy.dma_start(out=ones_f, in_=dr["c_ones_f"]))
        qI.dma(sy.dma_start(out=beta_s, in_=dr["beta"]))
        qI.dma(sy.dma_start(out=gbc_s, in_=dr["gbc_all"]))
        qI.dma(sy.dma_start(out=cc_s, in_=dr["cc"]))
        qI.dma(sy.dma_start(out=nw32, in_=dr["nw"].rearrange("(a b) -> a b", b=128)))
        qI.dma(sy.dma_start(out=maskc, in_=dr["mask"]))

    @blk.tensor
    def _(t):
        qI.wait(t)
        tp.inc(t.matmul(ps[0][:, 0:128], nw32, ident_f[0:32, :], start=True, stop=True))

    @blk.scalar
    def _(s):
        tp.wait(s)
        _w(s, sp) or sp.inc(s.copy(out=nw_s, in_=ps[0][:, 0:128].rearrange("p (a b) -> p a b", a=4)[:, 0, 0:32]))

    @blk.vector
    def _(v):
        qI.wait(v)
        _w(v, vp) or vp.inc(v.memset(S_in, 0.0))
        _w(v, vp) or vp.inc(v.memset(gamp, 1.0))
        _w(v, vp) or vp.inc(v.memset(epsc, EPS))

    for j in range(7):
        pp = j % 2

        @blk.sync
        def _(sy, j=j, pp=pp):
            if j >= 2:
                sy.wait_ge(vp.h, ev[("pc", j - 2)])
            qA[pp].dma(sy.dma_start(out=sj[pp], in_=dr["s_all"][j]))
            qA[pp].dma(sy.dma_start(out=gj[pp], in_=dr["gam_all"][j + 1]))

        @blk.vector
        def _(v, j=j, pp=pp):
            qA[pp].wait(v)
            _w(v, vp) or vp.inc(v.scalar_tensor_tensor(out=S_in, in0=sj[pp], scalar=maskc[:, j:j + 1],
                                          in1=S_in, op0=OP.mult, op1=OP.add))
            _w(v, vp) or vp.inc(v.tensor_scalar(out=gmj, in0=gj[pp], scalar1=maskc[:, j + 1:j + 2],
                                   scalar2=maskc[:, 9 + j:10 + j], op0=OP.mult, op1=OP.add))
            ev[("pc", j)] = _w(v, vp) or vp.inc(v.tensor_tensor(
                out=S_in.rearrange("p (h d) -> p h d", h=HEADS), 
                in0=S_in.rearrange("p (h d) -> p h d", h=HEADS),
                in1=gmj[:, :, None].broadcast_to([128, HEADS, PD]), op=OP.mult))

    @blk.vector
    def _(v):
        _w(v, vp) or vp.inc(v.tensor_copy(S_bf, S_in))

    barrier()

    for tt in range(NT):
        pp = tt % 2

        @blk.sync
        def _(sy, tt=tt, pp=pp):
            if tt >= 2:
                sy.wait_ge(tp.h, ev[("tr", tt - 2)])
            qY[pp].dma(sy.dma_start(out=y_t[pp], in_=dr["y_sp"][tt]))
            qY[pp].dma(sy.dma_start(out=sg_t[pp], in_=dr["sg_sp"][tt]))

        @blk.vector
        def _(v, tt=tt):
            _w(v, vp) or vp.inc(v.tensor_tensor(out=wq, in0=beta_s[:, tt, :], in1=gamp, op=OP.mult))

        @blk.vector
        def _(v, tt=tt):
            ev[("gp", tt)] = _w(v, vp) or vp.inc(v.tensor_tensor(out=gamp, in0=gamp, in1=gbc_s[:, tt, :], op=OP.mult))

        for g8 in range(8):
            pb = g8 % 2

            @blk.tensor
            def _(t, tt=tt, g8=g8, pb=pb):
                if g8 >= 2:
                    t.wait_ge(vp.h, ev[("co", tt, g8 - 2)])
                elif tt >= 1:
                    t.wait_ge(vp.h, ev[("co", tt - 1, g8 + 6)])
                tp.inc(t.matmul(ps[pb], cc_s[:, 128 * tt:128 * tt + 128],
                                S_bf[:, 512 * g8:512 * g8 + 512], start=True, stop=True))

            @blk.vector
            def _(v, tt=tt, g8=g8, pb=pb, pp=pp):
                tp.wait(v)
                qY[pp].wait(v)
                _w(v, vp) or vp.inc(v.tensor_tensor(
                    out=scr[:, 0:512].rearrange("p (a c) -> p a c", a=8),
                    in0=ps[pb].rearrange("p (a c) -> p a c", a=8),
                    in1=wq[:, 8 * g8:8 * g8 + 8, None].broadcast_to([128, 8, PD]), op=OP.mult))
                ev[("co", tt, g8)] = _w(v, vp) or vp.inc(v.tensor_tensor(
                    out=y_t[pp][:, 512 * g8:512 * g8 + 512],
                    in0=y_t[pp][:, 512 * g8:512 * g8 + 512], in1=scr[:, 0:512], op=OP.add))

        @blk.vector
        def _(v, tt=tt, pp=pp):
            _w(v, vp) or vp.inc(v.tensor_tensor(out=y_t[pp], in0=y_t[pp], in1=sg_t[pp], op=OP.mult))
            _w(v, vp) or vp.inc(v.scalar_tensor_tensor(out=scr, in0=y_t[pp], scalar=1.0, in1=y_t[pp],
                                          op0=OP.mult, op1=OP.mult, accum_out=mv[:, 0:1]))

        @blk.scalar
        def _(s, tt=tt):
            vp.wait(s)
            _w(s, sp) or sp.inc(s.activation(out=mv[:, 1:2], in_=mv[:, 0:1], func=AF.Ln, bias=epsc, scale=1.0 / INTER))
            _w(s, sp) or sp.inc(s.activation(out=mv[:, 2:3], in_=mv[:, 1:2], func=AF.Exp, scale=-0.5))

        @blk.vector
        def _(v, tt=tt, pp=pp):
            sp.wait(v)
            _w(v, vp) or vp.inc(v.tensor_scalar(out=y_t[pp], in0=y_t[pp], scalar1=mv[:, 2:3], scalar2=None, op0=OP.mult))
            ev[("n3", tt)] = vp.n

        for half in range(4):
            bk = 2 + half % 2

            @blk.tensor
            def _(t, tt=tt, pp=pp, half=half, bk=bk):
                t.wait_ge(vp.h, ev[("ntev", tt, half - 2)] if half >= 2 else ev[("n3", tt)])
                mm = None
                for k8 in range(8):
                    k2 = 8 * half + k8
                    mm = t.transpose(psB[bk][:, k8 * 128:k8 * 128 + 128],
                                     y_t[pp][:, 128 * k2:128 * k2 + 128], ident_b)
                ev[("trh", tt, half)] = tp.inc(mm)
                if half == 3:
                    ev[("tr", tt)] = tp.n

            @blk.vector
            def _(v, tt=tt, half=half, bk=bk):
                v.wait_ge(tp.h, ev[("trh", tt, half)])
                for k8 in range(8):
                    k2 = 8 * half + k8
                    _w(v, vp) or vp.inc(v.tensor_scalar(out=normedT[:, k2, 128 * tt:128 * tt + 128],
                                           in0=psB[bk][:, k8 * 128:k8 * 128 + 128],
                                           scalar1=nw_s[:, k2:k2 + 1], scalar2=None, op0=OP.mult))
                ev[("ntev", tt, half)] = vp.n

    barrier()

    wo_used = {}
    for oo in range(4):
        for k2 in range(NKI):
            idx = oo * NKI + k2
            pp = idx % 2

            @blk.gpsimd
            def _(g, oo=oo, k2=k2, pp=pp, idx=idx):
                if idx >= 2:
                    g.wait_ge(tp.h, wo_used[idx - 2])
                qV[pp].dma(g.dma_start(out=wob[pp], in_=dr["wout"][128 * k2:128 * k2 + 128,
                                                                   512 * oo:512 * oo + 512]))

            @blk.tensor
            def _(t, oo=oo, k2=k2, pp=pp, idx=idx):
                qV[pp].wait(t)
                if k2 == 0 and oo > 0:
                    t.wait_ge(vp.h, ev[("e4", oo - 1)])
                mm = None
                for tt in range(NT):
                    mm = t.matmul(ps[tt], normedT[:, k2, 128 * tt:128 * tt + 128], wob[pp],
                                  start=(k2 == 0), stop=(k2 == NKI - 1))
                wo_used[idx] = tp.inc(mm)

        for tt in range(NT):
            pp = tt % 2
            it = oo * NT + tt

            @blk.sync
            def _(sy, oo=oo, tt=tt, pp=pp, it=it):
                if it >= 2:
                    sy.wait_ge(gpp.h, ev[("go", it - 2)])
                qR[pp].dma(sy.dma_start(out=x_sl[pp], in_=dr["x"][128 * tt:128 * tt + 128,
                                                                  512 * oo:512 * oo + 512]))
                qR[pp].dma(sy.dma_start(out=g_sl[pp], in_=dr["g_sp"][tt][:, 512 * oo:512 * oo + 512]))

            @blk.vector
            def _(v, oo=oo, tt=tt, pp=pp, it=it):
                tp.wait(v)
                qR[pp].wait(v)
                if it >= 2:
                    v.wait_ge(gpp.h, ev[("go", it - 2)])
                _w(v, vp) or vp.inc(v.tensor_tensor(out=r_sl[pp], in0=ps[tt], in1=g_sl[pp], op=OP.mult))

            @blk.gpsimd
            def _(g, oo=oo, tt=tt, pp=pp, it=it):
                vp.wait(g)
                g.wait_ge(qO[pp].h, qO[pp].n)
                _w(g, gpp) or gpp.inc(g.tensor_tensor(out=o_sl[pp], in0=r_sl[pp], in1=x_sl[pp], op=OP.add))
                _w(g, gpp) or qO[pp].dma(g.dma_start(out=dr["out"][128 * tt:128 * tt + 128, 512 * oo:512 * oo + 512],
                                   in_=o_sl[pp]))
                ev[("go", it)] = gpp.n

        @blk.vector
        def _(v, oo=oo):
            ev[("e4", oo)] = vp.n

    @blk.sync
    def _(sy):
        qO[0].wait(sy)
        qO[1].wait(sy)

    barrier()
    ctx.close()
    return nc



_CACHE = {}


def _get_l1(debug=False):
    key = ("l1", debug)
    if key not in _CACHE:
        _CACHE[key] = build_l1(debug)
    return _CACHE[key]


def _get_l2():
    if "l2" not in _CACHE:
        _CACHE["l2"] = build_l2()
    return _CACHE["l2"]


def kernel(L0_attn_hidden_states, L0_attn_W_z, L0_attn_W_xBC, L0_attn_W_dt,
           L0_attn_conv_w, L0_attn_conv_b, L0_attn_dt_bias, L0_attn_A_log,
           L0_attn_D, L0_attn_norm_w, L0_attn_W_out, L0_ffn_factors, L0_ffn_bias):
    x = np.ascontiguousarray(np.asarray(L0_attn_hidden_states, dtype=np.float32)[0])
    consts = host_consts()
    common = {
        "wz": np.ascontiguousarray(L0_attn_W_z, dtype=np.float32),
        "wx": np.ascontiguousarray(L0_attn_W_xBC, dtype=np.float32),
        "wdt": np.ascontiguousarray(L0_attn_W_dt, dtype=np.float32),
        "convw": np.ascontiguousarray(L0_attn_conv_w, dtype=np.float32),
        "convb": np.ascontiguousarray(L0_attn_conv_b, dtype=np.float32),
        "dtbias": np.ascontiguousarray(L0_attn_dt_bias, dtype=np.float32),
        "alog": np.ascontiguousarray(L0_attn_A_log, dtype=np.float32),
        "dvec": np.ascontiguousarray(L0_attn_D, dtype=np.float32),
        "fac": np.ascontiguousarray(L0_ffn_factors, dtype=np.float32),
        "fbias": np.ascontiguousarray(L0_ffn_bias, dtype=np.float32),
    }
    common.update(consts)

    xpad = np.concatenate([np.zeros((HALO, DM), np.float32), x], 0)
    in1 = []
    for c in range(8):
        m = dict(common)
        m["x"] = np.ascontiguousarray(xpad[LSLAB * c: LSLAB * c + LT])
        in1.append(m)

    nc1 = _get_l1(DEBUG)
    r1 = run_bass_kernel_spmd(nc1, in1, core_ids=list(range(8)))
    res1 = r1.results

    s_all = np.stack([res1[c]["e_s"] for c in range(8)], 0)
    gam_all = np.stack([res1[c]["e_gam"] for c in range(8)], 0)

    common2 = {
        "wout": np.ascontiguousarray(L0_attn_W_out, dtype=np.float32),
        "nw": np.ascontiguousarray(L0_attn_norm_w, dtype=np.float32),
        "s_all": s_all,
        "gam_all": gam_all,
    }
    common2.update(consts)
    in2 = []
    for c in range(8):
        m = dict(common2)
        m["x"] = np.ascontiguousarray(x[LSLAB * c: LSLAB * (c + 1)])
        mask = np.zeros(16, np.float32)
        mask[0:c] = 1.0
        mask[8:16] = 1.0 - mask[0:8]
        m["mask"] = np.tile(mask[None, :], (128, 1)).astype(np.float32)
        m["beta"] = res1[c]["e_beta"]
        m["gbc_all"] = res1[c]["e_gbc"]
        m["cc"] = res1[c]["e_cc"]
        m["y_sp"] = res1[c]["e_y"]
        m["sg_sp"] = res1[c]["e_sg"]
        m["g_sp"] = res1[c]["e_g"]
        in2.append(m)

    nc2 = _get_l2()
    r2 = run_bass_kernel_spmd(nc2, in2, core_ids=list(range(8)))
    out = np.concatenate([r2.results[c]["out"] for c in range(8)], 0)
    return out[None, :, :].astype(np.float32)
